# version 03 (final)
# baseline (speedup 1.0000x reference)
import numpy as np
import ml_dtypes
import concourse.bass as bass
import concourse.mybir as mybir
from concourse.bass_utils import run_bass_kernel_spmd

F32 = mybir.dt.float32
BF16 = mybir.dt.bfloat16
I16 = mybir.dt.int16
I8 = mybir.dt.int8
AF = mybir.ActivationFunctionType
ALU = mybir.AluOpType
BF = ml_dtypes.bfloat16

NCORE = 8
HID = 2048
INTER = 4096
NH = 64
HD = 64
NST = 128
KCONV = 4
CONV_DIM = INTER + 2 * NST
PROJ = INTER + CONV_DIM + NH
NB = 2
SEQ = 4096
TOK = NB * SEQ
H8 = NH // NCORE
BIG = 1.0e7
LCH = 128

HS0, HS1 = 0, 512
B0, B1 = 512, 528
C0, C1 = 528, 544
DT0, DT1 = 544, 552
GT0, GT1 = 640, 1152
FCORE = 1152


def perm_h(n):
    out = []
    for j in range(0, n, 256):
        out.append(np.arange(j, j + 256, 2))
        out.append(np.arange(j + 1, j + 256, 2))
    return np.concatenate(out)


class Counter:

    def __init__(self, sem, step=1):
        self.sem = sem
        self.val = 0
        self.step = step

    def inc(self, instr, n=None):
        n = self.step if n is None else n
        instr.then_inc(self.sem, n)
        self.val += n
        return self.val


class Ctx:

    def __init__(self, nc, block):
        self.nc = nc
        self.block = block
        self.serial = False

    def _bar(self, eng, skip):
        if self.serial:
            for c in (self.ct, self.cv, self.cs):
                if c is not skip:
                    eng.wait_ge(c.sem, c.val)

    def T(self, f):
        def g(e):
            self._bar(e, self.ct)
            f(e)
        self.block.tensor(g)

    def V(self, f):
        def g(e):
            self._bar(e, self.cv)
            f(e)
        self.block.vector(g)

    def S(self, f):
        def g(e):
            self._bar(e, self.cs)
            f(e)
        self.block.scalar(g)

    def G(self, f):
        self.block.gpsimd(f)

    def Y(self, f):
        self.block.sync(f)

    def dy(self, eng, out, in_):
        eng.dma_start(out=out, in_=in_).then_inc(self.cy.sem, 16)
        self.cy.val += 16

    def dg(self, eng, out, in_):
        eng.dma_start(out=out, in_=in_).then_inc(self.cg.sem, 16)
        self.cg.val += 16




def tr(tt, out, in_, ident, start):
    return tt.matmul(out, in_, ident, is_transpose=True, start=start, stop=False)



def emit_scan_prep(cx, t):
    nc, TOKL = cx.nc, t["TOKL"]
    ct, cv, cs = cx.ct, cx.cv, cx.cs

    def s_prep(s):
        s.wait_ge(cv.sem, t["prep_wait_v"])
        s.wait_ge(cx.cg.sem, t["prep_wait_g"])
        if "prep_wait_y" in t:
            s.wait_ge(cx.cy.sem, t["prep_wait_y"])
        s.activation(t["acol"][:], t["alog"][:], AF.Exp)
        s.activation(t["sp1"][:], t["dtraw"][:], AF.Abs, bias=t["bias8h"][:, 0:1])
        s.activation(t["sp1"][:], t["sp1"][:], AF.Exp, scale=t["cm1"][0:8, 0:1])
        s.activation(t["sp1"][:], t["sp1"][:], AF.Ln, bias=1.0)
        cs.inc(s.activation(t["dt"][:], t["dtraw"][:], AF.Relu, bias=t["bias8h"][:, 0:1]))

    cx.S(s_prep)

    def v_prep(v):
        v.wait_ge(cs.sem, cs.val)
        v.tensor_scalar(t["acol"][:], t["acol"][:], -1.0, None, ALU.mult)
        v.tensor_add(t["dt"][:], t["dt"][:], t["sp1"][:])
        cv.inc(v.tensor_scalar(t["a8"][:], t["dt"][:], t["acol"][:, 0:1], None, ALU.mult))

    cx.V(v_prep)

    def s_prep2(s):
        s.wait_ge(cv.sem, cv.val)
        cs.inc(s.activation(t["lndt8"][:], t["dt"][:], AF.Ln))

    cx.S(s_prep2)

    def t_prep(tt):
        tt.wait_ge(cx.cg.sem, t["prep_wait_g"])
        ct.inc(tt.matmul(cx.pb[0][:, 0:8], t["ones1f"][:], t["drow"][:], start=True, stop=True))

    cx.T(t_prep)

    def v_prep2(v):
        v.wait_ge(ct.sem, ct.val)
        cv.inc(v.tensor_copy(t["db"][:], cx.pb[0][:, 0:8]))

    cx.V(v_prep2)

    def v_prep3(v):
        v.wait_ge(cv.sem, cv.val)
        for h in range(H8):
            ins = v.tensor_scalar(t["id8"][:, 128 * h:128 * h + 128], t["idbf"][:],
                                  t["db"][:, h:h + 1], None, ALU.mult)
        cv.inc(ins)

    cx.V(v_prep3)


def emit_scan(cx, t):
    nc, TOKL, SEQL = cx.nc, t["TOKL"], t["SEQL"]
    ct, cv, cs = cx.ct, cx.cv, cx.cs
    pb = cx.pb
    nchunks = SEQL // LCH
    pb0b = pb[0].ap().bitcast(BF16)
    pb2b = pb[2].ap().bitcast(BF16)
    a8, lndt8 = t["a8"], t["lndt8"]
    hs4, Bres, Cres, y4 = t["hs4"], t["Bres"], t["Cres"], t["y4"]

    gv = t.get("scan_wait_g", cx.cg.val)

    def t_gate(tt):
        tt.wait_ge(cx.cg.sem, gv)
    cx.T(t_gate)

    def v_gate(v):
        v.wait_ge(cx.cg.sem, gv)
    cx.V(v_gate)

    def s_gate(s):
        s.wait_ge(cx.cg.sem, gv)
    cx.S(s_gate)

    for b in range(NB):
        def v_zero(v):
            v.wait_ge(ct.sem, ct.val)
            v.memset(t["S"][:], 0.0)
            cv.inc(v.memset(t["Sbf"][:], 0.0))
        cx.V(v_zero)

        for c in range(nchunks):
            cs_off = b * SEQL + c * LCH
            csl = slice(cs_off, cs_off + LCH)
            prev_v, prev_s = cv.val, cs.val

            def pe1(tt, csl=csl, wv=prev_v, ws=prev_s):
                tt.wait_ge(cv.sem, wv)
                tt.wait_ge(cs.sem, ws)
                tr(tt, pb[0][:, 0:8], a8[:, csl], t["idf"][0:8, 0:8], True)
                ct.inc(tr(tt, pb[0][:, 8:16], lndt8[:, csl], t["idf"][0:8, 0:8], False))
            cx.T(pe1)
            pe1_t = ct.val

            def pe4(tt, csl=csl):
                ct.inc(tt.matmul(pb[2][:, 0:128], Bres[:, csl], Cres[:, csl], start=True, stop=False))
                tr(tt, pb2b[:, 256:384], Bres[:, csl], t["idbf"][:], False)
                for r in range(4):
                    ins = tr(tt, pb2b[:, 512 + 128 * r:640 + 128 * r],
                             hs4[:, r * TOKL + cs_off: r * TOKL + cs_off + LCH],
                             t["idbf"][:], False)
                ct.inc(ins)
            cx.T(pe4)
            pe4_t = ct.val

            def v1(v):
                v.wait_ge(ct.sem, pe1_t)
                cv.inc(v.tensor_copy(t["T16"][:], pb[0][:, 0:16]))
            cx.V(v1)
            v1_v = cv.val

            def pe2(tt):
                tt.wait_ge(cv.sem, v1_v)
                ct.inc(tt.matmul(pb[1][:, 0:8], t["utf"][:], t["T16"][:, 0:8], start=True, stop=True))
            cx.T(pe2)
            pe2_t = ct.val

            def v2(v):
                v.wait_ge(ct.sem, pe2_t)
                cv.inc(v.tensor_copy(t["ca"][:], pb[1][:, 0:8]))
            cx.V(v2)
            v2_v = cv.val

            def pe3(tt):
                tt.wait_ge(cv.sem, v2_v)
                ct.inc(tr(tt, pb[1][:, 16:144][0:8, :], t["ca"][:], t["idf"][:], False))
            cx.T(pe3)
            pe3_t = ct.val

            def v3a(v):
                v.wait_ge(ct.sem, pe3_t)
                cv.inc(v.tensor_copy(t["caTT"][0:8, :], pb[1][:, 16:144][0:8, :]))
            cx.V(v3a)
            v3a_v = cv.val

            def v3(v):
                v.wait_ge(cv.sem, v3a_v)
                v.tensor_sub(t["bias8"][:], t["T16"][:, 8:16], t["ca"][:])
                cab = t["caTT"][0:8, :].unsqueeze(1).broadcast_to([8, 8, 128])
                cv.inc(v.tensor_tensor(t["rhsD"][:].rearrange("p (a r) -> p a r", a=8),
                                       t["selm"][0:8, :].rearrange("p (a r) -> p a r", a=8),
                                       cab, ALU.mult))
            cx.V(v3)
            v3_v = cv.val

            def s1(s):
                s.wait_ge(cv.sem, v3_v)
                cs.inc(s.activation(t["epos"][:], t["ca"][:], AF.Exp))
            cx.S(s1)
            s1_s = cs.val

            def v4a(v):
                v.wait_ge(ct.sem, pe4_t)
                v.tensor_copy(t["CBTT"][:], pb[2][:, 0:128])
                v.tensor_copy(t["BT"][:], pb2b[:, 256:384])
                cv.inc(v.tensor_copy(t["hsT"][:], pb2b[:, 512:1024]))
            cx.V(v4a)
            v4a_v = cv.val

            def pe5(tt):
                tt.wait_ge(cv.sem, v3_v)
                for bank, hh in ((4, 0), (5, 4)):
                    tt.matmul(pb[bank][:], t["idbf"][:], t["slbig"][:], start=True, stop=False)
                    ins = tt.matmul(pb[bank][:], t["ones8f"][:],
                                    t["rhsD"][:, 512 * (hh // 4):512 * (hh // 4) + 512],
                                    start=False, stop=False)
                ct.inc(ins)
            cx.T(pe5)
            pe5_t = ct.val

            def v3b(v):
                v.wait_ge(ct.sem, ct.val)
                for bank, hh in ((4, 0), (5, 4)):
                    col = pb[bank][:, 127:512:128]
                    ins = v.tensor_add(t["F2e"][:, hh:hh + 4], t["bias8"][:, hh:hh + 4], col)
                cv.inc(ins)
            cx.V(v3b)
            v3b_v = cv.val

            def s1b(s):
                s.wait_ge(cv.sem, v3b_v)
                s.activation(t["F2"][:], t["F2e"][:], AF.Exp)
                for bank, hh in ((4, 0), (5, 4)):
                    col = pb[bank][:, 127:512:128]
                    ins = s.activation(t["ecaL"][:, hh:hh + 4], col, AF.Exp)
                cs.inc(ins)
            cx.S(s1b)
            s1b_s = cs.val

            def v4b(v):
                v.wait_ge(cs.sem, s1b_s)
                f2b = t["F2"][:, :].unsqueeze(2).broadcast_to([128, 8, 64])
                cv.inc(v.tensor_tensor(t["dtxw"][:].rearrange("p (a r) -> p a r", a=8),
                                       t["hsT"][:].rearrange("p (a r) -> p a r", a=8),
                                       f2b, ALU.mult))
            cx.V(v4b)
            v4b_v = cv.val

            def s2(s):
                s.wait_ge(ct.sem, pe5_t)
                for bank, hh in ((4, 0), (5, 4)):
                    for j in range(4):
                        h = hh + j
                        ins = s.activation(t["M4"][:, 128 * h:128 * h + 128],
                                           pb[bank][:, 128 * j:128 * j + 128],
                                           AF.Exp, bias=t["bias8"][:, h:h + 1])
                cs.inc(ins)
            cx.S(s2)
            s2_s = cs.val

            def v5(v):
                v.wait_ge(cs.sem, s2_s)
                v.wait_ge(cv.sem, v4a_v)
                cb = t["CBTT"][:, :].unsqueeze(1).broadcast_to([128, 8, 128])
                cv.inc(v.tensor_tensor(t["G4"][:].rearrange("p (a r) -> p a r", a=8),
                                       t["M4"][:].rearrange("p (a r) -> p a r", a=8),
                                       cb, ALU.mult))
            cx.V(v5)
            v5_v = cv.val

            def pe6a(tt, csl=csl):
                for h in range(H8):
                    ins = tt.matmul(pb[7][:, 64 * h:64 * h + 64], Cres[:, csl],
                                    t["Sbf"][:, 64 * h:64 * h + 64], start=(h == 0), stop=False)
                ct.inc(ins)
            cx.T(pe6a)
            pe6a_t = ct.val

            def pe6b(tt):
                tt.wait_ge(cv.sem, v5_v)
                for h in range(H8):
                    tt.matmul(pb[6][:, 64 * h:64 * h + 64], t["G4"][:, 128 * h:128 * h + 128],
                              t["hsT"][:, 64 * h:64 * h + 64], start=(h == 0), stop=False)
                for h in range(H8):
                    ins = tt.matmul(pb[6][:, 64 * h:64 * h + 64], t["id8"][:, 128 * h:128 * h + 128],
                                    t["hsT"][:, 64 * h:64 * h + 64], start=False, stop=False)
                ct.inc(ins)
            cx.T(pe6b)
            pe6b_t = ct.val

            def pe6c(tt):
                tt.wait_ge(cv.sem, v4b_v)
                tt.wait_ge(cs.sem, s2_s)
                for h in range(H8):
                    ins = tt.matmul(pb[5][:, 64 * h:64 * h + 64], t["BT"][:],
                                    t["dtxw"][:, 64 * h:64 * h + 64], start=(h == 0), stop=False)
                ct.inc(ins)
            cx.T(pe6c)
            pe6c_t = ct.val

            def v6(v):
                v.wait_ge(ct.sem, pe6b_t)
                eb = t["epos"][:, :].unsqueeze(2).broadcast_to([128, 8, 64])
                v.tensor_tensor(t["t1"][:].rearrange("p (a r) -> p a r", a=8),
                                pb[7][:].rearrange("p (a r) -> p a r", a=8), eb, ALU.mult)
                cv.inc(v.tensor_add(t["Ysb"][:], t["t1"][:], pb[6][:]))
            cx.V(v6)
            v6_v = cv.val

            def v7(v):
                v.wait_ge(ct.sem, pe6c_t)
                for h in range(H8):
                    v.scalar_tensor_tensor(t["S"][:, 64 * h:64 * h + 64],
                                           t["S"][:, 64 * h:64 * h + 64],
                                           t["ecaL"][:, h:h + 1],
                                           pb[5][:, 64 * h:64 * h + 64],
                                           ALU.mult, ALU.add)
                cv.inc(v.tensor_copy(t["Sbf"][:], t["S"][:]))
            cx.V(v7)

            def pe7(tt):
                tt.wait_ge(cv.sem, v6_v)
                for r in range(4):
                    ins = tr(tt, pb0b[:, 128 * r:128 * r + 128],
                             t["Ysb"][:, 128 * r:128 * r + 128], t["idbf"][:], r == 0)
                ct.inc(ins)
            cx.T(pe7)
            pe7_t = ct.val

            def s3(s, cs_off=cs_off):
                s.wait_ge(ct.sem, pe7_t)
                out = y4[:, :].rearrange("p (r n) -> p r n", r=4)[:, :, cs_off:cs_off + LCH]
                cs.inc(s.activation(out, pb0b[:, 0:512].rearrange("p (r n) -> p r n", r=4),
                                    AF.Copy))
            cx.S(s3)



def emit_scan_stream(cx, t, d_hsconv, d_y, hs4a, hs4b, ysta, ystb):
    nc, TOKL, SEQL = cx.nc, t["TOKL"], t["SEQL"]
    ct, cv, cs = cx.ct, cx.cv, cx.cs
    cy, semY = cx.cy, cx.cy.sem
    pb = cx.pb
    nchunks = SEQL // LCH
    pb0b = pb[0].ap().bitcast(BF16)
    pb2b = pb[2].ap().bitcast(BF16)
    a8, lndt8 = t["a8"], t["lndt8"]
    Bres, Cres = t["Bres"], t["Cres"]

    gv = cx.cg.val
    gy = cx.cy.val

    def t_gate(tt):
        tt.wait_ge(cx.cg.sem, gv)
        tt.wait_ge(cx.cy.sem, gy)
    cx.T(t_gate)

    def v_gate(v):
        v.wait_ge(cx.cg.sem, gv)
        v.wait_ge(cx.cy.sem, gy)
    cx.V(v_gate)

    def s_gate(s):
        s.wait_ge(cx.cg.sem, gv)
        s.wait_ge(cx.cy.sem, gy)
    cx.S(s_gate)

    pe4_hist = []
    st_hist = []

    for b in range(NB):
        def v_zero(v):
            v.wait_ge(ct.sem, ct.val)
            v.memset(t["S"][:], 0.0)
            cv.inc(v.memset(t["Sbf"][:], 0.0))
        cx.V(v_zero)

        for c in range(nchunks):
            gc = b * nchunks + c
            cs_off = b * SEQL + c * LCH
            csl = slice(cs_off, cs_off + LCH)
            hsbuf = (hs4a, hs4b)[gc % 2]
            ystg = (ysta, ystb)[gc % 2]
            prev_v, prev_s = cv.val, cs.val

            def y_hs(sy, gc=gc, hsbuf=hsbuf, cs_off=cs_off):
                if gc >= 2:
                    sy.wait_ge(ct.sem, pe4_hist[gc - 2])
                sy.dma_start(out=hsbuf[:].rearrange("p (r n) -> p r n", r=4),
                             in_=d_hsconv.rearrange("(r p) n -> p r n", p=128)[:, :, cs_off:cs_off + LCH]).then_inc(semY, 16)
            cx.Y(y_hs)
            cy.val += 16
            hs_y = cy.val

            def pe1(tt, csl=csl, wv=prev_v, ws=prev_s):
                tt.wait_ge(cv.sem, wv)
                tt.wait_ge(cs.sem, ws)
                tr(tt, pb[0][:, 0:8], a8[:, csl], t["idf"][0:8, 0:8], True)
                ct.inc(tr(tt, pb[0][:, 8:16], lndt8[:, csl], t["idf"][0:8, 0:8], False))
            cx.T(pe1)
            pe1_t = ct.val

            def pe4(tt, csl=csl, hsbuf=hsbuf, hs_y=hs_y):
                tt.wait_ge(semY, hs_y)
                ct.inc(tt.matmul(pb[2][:, 0:128], Bres[:, csl], Cres[:, csl], start=True, stop=False))
                tr(tt, pb2b[:, 256:384], Bres[:, csl], t["idbf"][:], False)
                for r in range(4):
                    ins = tr(tt, pb2b[:, 512 + 128 * r:640 + 128 * r],
                             hsbuf[:, 128 * r:128 * r + 128], t["idbf"][:], False)
                ct.inc(ins)
            cx.T(pe4)
            pe4_t = ct.val
            pe4_hist.append(pe4_t)

            def v1(v, pe1_t=pe1_t):
                v.wait_ge(ct.sem, pe1_t)
                cv.inc(v.tensor_copy(t["T16"][:], pb[0][:, 0:16]))
            cx.V(v1)
            v1_v = cv.val

            def pe2(tt, v1_v=v1_v):
                tt.wait_ge(cv.sem, v1_v)
                ct.inc(tt.matmul(pb[1][:, 0:8], t["utf"][:], t["T16"][:, 0:8], start=True, stop=False))
            cx.T(pe2)
            pe2_t = ct.val

            def v2(v, pe2_t=pe2_t):
                v.wait_ge(ct.sem, pe2_t)
                cv.inc(v.tensor_copy(t["ca"][:], pb[1][:, 0:8]))
            cx.V(v2)
            v2_v = cv.val

            def pe3(tt, v2_v=v2_v):
                tt.wait_ge(cv.sem, v2_v)
                ct.inc(tr(tt, pb[1][:, 16:144][0:8, :], t["ca"][:], t["idf"][:], False))
            cx.T(pe3)
            pe3_t = ct.val

            def v3a(v, pe3_t=pe3_t):
                v.wait_ge(ct.sem, pe3_t)
                cv.inc(v.tensor_copy(t["caTT"][0:8, :], pb[1][:, 16:144][0:8, :]))
            cx.V(v3a)
            v3a_v = cv.val

            def v3(v, v3a_v=v3a_v):
                v.wait_ge(cv.sem, v3a_v)
                v.tensor_sub(t["bias8"][:], t["T16"][:, 8:16], t["ca"][:])
                v.wait_ge(cv.sem, v3a_v)
                cab = t["caTT"][0:8, :].unsqueeze(1).broadcast_to([8, 8, 128])
                cv.inc(v.tensor_tensor(t["rhsD"][:].rearrange("p (a r) -> p a r", a=8),
                                       t["selm"][0:8, :].rearrange("p (a r) -> p a r", a=8),
                                       cab, ALU.mult))
            cx.V(v3)
            v3_v = cv.val

            def s1(s, v3_v=v3_v):
                s.wait_ge(cv.sem, v3_v)
                cs.inc(s.activation(t["epos"][:], t["ca"][:], AF.Exp))
            cx.S(s1)
            s1_s = cs.val

            def v4a(v, pe4_t=pe4_t):
                v.wait_ge(ct.sem, pe4_t)
                v.tensor_copy(t["CBTT"][:], pb[2][:, 0:128])
                v.tensor_copy(t["BT"][:], pb2b[:, 256:384])
                cv.inc(v.tensor_copy(t["hsT"][:], pb2b[:, 512:1024]))
            cx.V(v4a)
            v4a_v = cv.val

            def pe5(tt, v3_v=v3_v):
                tt.wait_ge(cv.sem, v3_v)
                for bank, hh in ((4, 0), (5, 4)):
                    tt.matmul(pb[bank][:], t["idbf"][:], t["slbig"][:], start=True, stop=False)
                    ins = tt.matmul(pb[bank][:], t["ones8f"][:],
                                    t["rhsD"][:, 512 * (hh // 4):512 * (hh // 4) + 512],
                                    start=False, stop=False)
                ct.inc(ins)
            cx.T(pe5)
            pe5_t = ct.val

            def v3b(v, wt=ct.val):
                v.wait_ge(ct.sem, wt)
                for bank, hh in ((4, 0), (5, 4)):
                    col = pb[bank][:, 127:512:128]
                    ins = v.tensor_add(t["F2e"][:, hh:hh + 4], t["bias8"][:, hh:hh + 4], col)
                cv.inc(ins)
            cx.V(v3b)
            v3b_v = cv.val

            def s1b(s, v3b_v=v3b_v):
                s.wait_ge(cv.sem, v3b_v)
                s.activation(t["F2"][:], t["F2e"][:], AF.Exp)
                for bank, hh in ((4, 0), (5, 4)):
                    col = pb[bank][:, 127:512:128]
                    ins = s.activation(t["ecaL"][:, hh:hh + 4], col, AF.Exp)
                cs.inc(ins)
            cx.S(s1b)
            s1b_s = cs.val

            def v4b(v, s1b_s=s1b_s):
                v.wait_ge(cs.sem, s1b_s)
                f2b = t["F2"][:, :].unsqueeze(2).broadcast_to([128, 8, 64])
                cv.inc(v.tensor_tensor(t["dtxw"][:].rearrange("p (a r) -> p a r", a=8),
                                       t["hsT"][:].rearrange("p (a r) -> p a r", a=8),
                                       f2b, ALU.mult))
            cx.V(v4b)
            v4b_v = cv.val

            def s2(s, pe5_t=pe5_t):
                s.wait_ge(ct.sem, pe5_t)
                for bank, hh in ((4, 0), (5, 4)):
                    for j in range(4):
                        h = hh + j
                        ins = s.activation(t["M4"][:, 128 * h:128 * h + 128],
                                           pb[bank][:, 128 * j:128 * j + 128],
                                           AF.Exp, bias=t["bias8"][:, h:h + 1])
                cs.inc(ins)
            cx.S(s2)
            s2_s = cs.val

            def v5(v, s2_s=s2_s, v4a_v=v4a_v):
                v.wait_ge(cs.sem, s2_s)
                v.wait_ge(cv.sem, v4a_v)
                cb = t["CBTT"][:, :].unsqueeze(1).broadcast_to([128, 8, 128])
                cv.inc(v.tensor_tensor(t["G4"][:].rearrange("p (a r) -> p a r", a=8),
                                       t["M4"][:].rearrange("p (a r) -> p a r", a=8),
                                       cb, ALU.mult))
            cx.V(v5)
            v5_v = cv.val

            def pe6a(tt, csl=csl):
                for h in range(H8):
                    ins = tt.matmul(pb[7][:, 64 * h:64 * h + 64], Cres[:, csl],
                                    t["Sbf"][:, 64 * h:64 * h + 64], start=(h == 0), stop=False)
                ct.inc(ins)
            cx.T(pe6a)

            def pe6b(tt, v5_v=v5_v):
                tt.wait_ge(cv.sem, v5_v)
                for h in range(H8):
                    tt.matmul(pb[6][:, 64 * h:64 * h + 64], t["G4"][:, 128 * h:128 * h + 128],
                              t["hsT"][:, 64 * h:64 * h + 64], start=(h == 0), stop=False)
                for h in range(H8):
                    ins = tt.matmul(pb[6][:, 64 * h:64 * h + 64], t["id8"][:, 128 * h:128 * h + 128],
                                    t["hsT"][:, 64 * h:64 * h + 64], start=False, stop=False)
                ct.inc(ins)
            cx.T(pe6b)
            pe6b_t = ct.val

            def pe6c(tt, v4b_v=v4b_v, s2_s=s2_s):
                tt.wait_ge(cv.sem, v4b_v)
                tt.wait_ge(cs.sem, s2_s)
                for h in range(H8):
                    ins = tt.matmul(pb[5][:, 64 * h:64 * h + 64], t["BT"][:],
                                    t["dtxw"][:, 64 * h:64 * h + 64], start=(h == 0), stop=False)
                ct.inc(ins)
            cx.T(pe6c)
            pe6c_t = ct.val

            def v6(v, pe6b_t=pe6b_t):
                v.wait_ge(ct.sem, pe6b_t)
                eb = t["epos"][:, :].unsqueeze(2).broadcast_to([128, 8, 64])
                v.tensor_tensor(t["t1"][:].rearrange("p (a r) -> p a r", a=8),
                                pb[7][:].rearrange("p (a r) -> p a r", a=8), eb, ALU.mult)
                cv.inc(v.tensor_add(t["Ysb"][:], t["t1"][:], pb[6][:]))
            cx.V(v6)
            v6_v = cv.val

            def v7(v, pe6c_t=pe6c_t):
                v.wait_ge(ct.sem, pe6c_t)
                for h in range(H8):
                    v.scalar_tensor_tensor(t["S"][:, 64 * h:64 * h + 64],
                                           t["S"][:, 64 * h:64 * h + 64],
                                           t["ecaL"][:, h:h + 1],
                                           pb[5][:, 64 * h:64 * h + 64],
                                           ALU.mult, ALU.add)
                cv.inc(v.tensor_copy(t["Sbf"][:], t["S"][:]))
            cx.V(v7)

            def pe7(tt, v6_v=v6_v):
                tt.wait_ge(cv.sem, v6_v)
                for r in range(4):
                    ins = tr(tt, pb0b[:, 128 * r:128 * r + 128],
                             t["Ysb"][:, 128 * r:128 * r + 128], t["idbf"][:], r == 0)
                ct.inc(ins)
            cx.T(pe7)
            pe7_t = ct.val

            def s3(s, gc=gc, pe7_t=pe7_t, ystg=ystg):
                s.wait_ge(ct.sem, pe7_t)
                if gc >= 2:
                    s.wait_ge(semY, st_hist[gc - 2])
                cs.inc(s.activation(ystg[:], pb0b[:, 0:512], AF.Copy))
            cx.S(s3)
            s3_s = cs.val

            def y_st(sy, cs_off=cs_off, ystg=ystg, s3_s=s3_s):
                sy.wait_ge(cs.sem, s3_s)
                sy.dma_start(out=d_y.rearrange("(r p) n -> p r n", p=128)[:, :, cs_off:cs_off + LCH],
                             in_=ystg[:].rearrange("p (r n) -> p r n", r=4)).then_inc(semY, 16)
            cx.Y(y_st)
            cy.val += 16
            st_hist.append(cy.val)




def emit_dequant_tile2(v, pkt, St, w_even, w_odd, tmp):
    nlo, nhi = tmp["nlo"], tmp["nhi"]
    u, sg_i, t1, sgf, ssg = tmp["u"], tmp["sgi"], tmp["t1"], tmp["sgf"], tmp["ssg"]
    v.tensor_scalar(nlo[:], pkt, 15, None, ALU.bitwise_and)
    v.tensor_scalar(nhi[:], pkt, 4, None, ALU.logical_shift_right)
    for n, wout in ((nlo, w_even), (nhi, w_odd)):
        v.tensor_scalar(u[:], n[:], 7, 6, ALU.bitwise_and, ALU.logical_shift_left)
        v.tensor_scalar(u[:], u[:], 0x3F00, None, ALU.add)
        raw = u[:].bitcast(BF16)
        v.tensor_scalar(t1[:], raw, 2.0, -1.0, ALU.mult, ALU.add)
        v.tensor_tensor(t1[:], t1[:], raw, ALU.min)
        v.tensor_scalar(sg_i[:], n[:], 8, 3, ALU.bitwise_and, ALU.logical_shift_right)
        v.tensor_copy(sgf[:], sg_i[:])
        v.tensor_scalar(sgf[:], sgf[:], -2.0, 1.0, ALU.mult, ALU.add)
        v.tensor_tensor(ssg[:], St, sgf[:], ALU.mult)
        v.tensor_tensor(wout, t1[:], ssg[:], ALU.mult)


def emit_dequant_mat(cx, d_pk, d_sc, cc_out, consts_y):
    nc = cx.nc
    ct, cv, cs, cg, cy = cx.ct, cx.cv, cx.cs, cx.cg, cx.cy
    semG, semY = cg.sem, cy.sem
    import contextlib
    with contextlib.ExitStack() as des:
        dsb = lambda n, s, d: des.enter_context(nc.sbuf_tensor(n, s, d))
        pkt = [dsb(f"dq_pk{i}", [128, 2048], I16) for i in range(2)]
        St = [dsb(f"dq_st{i}", [128, 2048], BF16) for i in range(2)]
        we = [dsb(f"dq_we{i}", [128, 2048], BF16) for i in range(2)]
        wo = [dsb(f"dq_wo{i}", [128, 2048], BF16) for i in range(2)]
        tmp = dict(nlo=dsb("dq_nlo", [128, 2048], I16), nhi=dsb("dq_nhi", [128, 2048], I16),
                   u=dsb("dq_u", [128, 2048], I16), sgi=dsb("dq_sgi", [128, 2048], I16),
                   t1=dsb("dq_t1", [128, 2048], BF16), sgf=dsb("dq_sgf", [128, 2048], BF16),
                   ssg=dsb("dq_ssg", [128, 2048], BF16))
        out_y = []
        for j in range(8):
            par = j % 2
            def g_ld(g, j=j, par=par):
                if j >= 2:
                    g.wait_ge(cv.sem, cv.val)
                g.dma_start(out=pkt[par][:], in_=d_pk[128 * j:128 * j + 128, :]).then_inc(semG, 16)
                for a in range(8):
                    g.dma_start(out=St[par][16 * a:16 * a + 16, :],
                                in_=d_sc[8 * j + a:8 * j + a + 1, :].broadcast_to([16, 2048])).then_inc(semG, 16)
            cx.G(g_ld)
            cg.val += 16 * 9
            ld_g = cg.val

            def v_dq(v, j=j, par=par, ld_g=ld_g):
                v.wait_ge(semG, ld_g)
                if j >= 2:
                    v.wait_ge(semY, out_y[j - 2])
                emit_dequant_tile2(v, pkt[par][:], St[par][:], we[par][:], wo[par][:], tmp)
                cv.inc(v.engine_nop())
            cx.V(v_dq)
            dq_v = cv.val

            def y_out(sy, j=j, par=par, dq_v=dq_v):
                sy.wait_ge(cv.sem, dq_v)
                o = cc_out.rearrange("hb m h -> m hb h")
                sy.dma_start(out=o[256 * j:256 * j + 128], in_=we[par][:].rearrange("p (hb h) -> p hb h", hb=16)).then_inc(semY, 16)
                sy.dma_start(out=o[256 * j + 128:256 * j + 256], in_=wo[par][:].rearrange("p (hb h) -> p hb h", hb=16)).then_inc(semY, 16)
            cx.Y(y_out)
            cy.val += 32
            out_y.append(cy.val)




def host_prep(inputs):
    PH = perm_h(HID)
    x = np.ascontiguousarray(np.asarray(inputs["L0_attn_hidden_states"]))
    xt = np.ascontiguousarray(x.reshape(TOK, HID).T)
    xtp = np.ascontiguousarray(xt[PH])
    wm = np.asarray(inputs["L0_attn_in_proj_w"])
    conv_w = np.asarray(inputs["L0_attn_conv_w"])[:, 0, :]
    conv_b = np.asarray(inputs["L0_attn_conv_b"])
    dt_bias = np.asarray(inputs["L0_attn_dt_bias"])
    a_log = np.asarray(inputs["L0_attn_A_log"])
    dd = np.asarray(inputs["L0_attn_D"])
    nw = np.asarray(inputs["L0_attn_norm_weight"])
    wo = np.asarray(inputs["L0_attn_out_proj_w"])
    gpk = np.asarray(inputs["L0_ffn_gate_W_packed"])
    gsc = np.asarray(inputs["L0_ffn_gate_scales"])
    upk = np.asarray(inputs["L0_ffn_up_W_packed"])
    usc = np.asarray(inputs["L0_ffn_up_scales"])
    dpk = np.asarray(inputs["L0_ffn_down_W_packed"])
    dsc = np.asarray(inputs["L0_ffn_down_scales"])
    PM = perm_h(2048)

    in_maps = []
    for c in range(NCORE):
        im = {}
        im["d_xtp"] = xtp
        im["d_xtok"] = np.ascontiguousarray(x.reshape(TOK, HID)[1024 * c:1024 * c + 1024])
        im["d_xres"] = np.ascontiguousarray(xt[:, 1024 * c:1024 * c + 1024])
        rows = np.concatenate([
            np.arange(4096 + 512 * c, 4096 + 512 * c + 512),
            np.arange(8192 + 16 * c, 8192 + 16 * c + 16),
            np.arange(8320 + 16 * c, 8320 + 16 * c + 16),
            np.arange(8448 + 8 * c, 8448 + 8 * c + 8),
        ])
        wsl = np.zeros((FCORE, HID), np.float32)
        wsl[0:552] = wm[rows]
        wsl[GT0:GT1] = wm[512 * c:512 * c + 512]
        im["d_wmt"] = np.ascontiguousarray(wsl.T[PH])
        ch = np.concatenate([np.arange(512 * c, 512 * c + 512),
                             np.arange(4096 + 16 * c, 4096 + 16 * c + 16),
                             np.arange(4224 + 16 * c, 4224 + 16 * c + 16)])
        im["d_convw"] = np.ascontiguousarray(conv_w[ch])
        im["d_convb"] = np.ascontiguousarray(conv_b[ch]).reshape(544, 1)
        im["d_dtbias"] = np.ascontiguousarray(dt_bias[8 * c:8 * c + 8]).reshape(8, 1)
        im["d_alog"] = np.ascontiguousarray(a_log[8 * c:8 * c + 8]).reshape(8, 1)
        im["d_drow"] = np.ascontiguousarray(dd[8 * c:8 * c + 8]).reshape(1, 8)
        im["d_nw4"] = np.ascontiguousarray(nw[512 * c:512 * c + 512].reshape(4, 128).T)
        im["d_wot"] = np.ascontiguousarray(wo.T)
        rsl = np.arange(2048 * c, 2048 * c + 2048)[PM]
        im["d_gpk"] = np.ascontiguousarray(gpk[rsl].T).view(np.int8)[:, 0::4].copy()
        im["d_gsc"] = np.ascontiguousarray(gsc[rsl].T)
        im["d_upk"] = np.ascontiguousarray(upk[rsl].T).view(np.int8)[:, 0::4].copy()
        im["d_usc"] = np.ascontiguousarray(usc[rsl].T)
        im["d_dpk"] = np.ascontiguousarray(dpk[:, 1024 * c:1024 * c + 1024].T).view(np.int8)[:, 0::4].copy()
        im["d_dsc"] = np.ascontiguousarray(dsc[:, 64 * c:64 * c + 64].T)
        im["d_idf"] = np.eye(128, dtype=np.float32)
        im["d_ones1"] = np.ones((1, 128), np.float32)
        im["d_ut"] = np.triu(np.ones((128, 128), np.float32))
        im["d_slbig"] = np.tile(np.tril(np.full((128, 128), -BIG, np.float32), -1), (1, 4))
        im["d_selm"] = np.repeat(np.eye(8, dtype=np.float32), 128, axis=1)
        in_maps.append(im)
    return in_maps


def host_finish(results):
    outs = [r["d_out"] for r in results]
    full = np.concatenate(outs, axis=1)
    return np.ascontiguousarray(full.T).reshape(NB, SEQ, HID)



def build_kernel():
    import contextlib, os
    PH = float(os.environ.get("MK_PHASES", "99"))
    nc = bass.Bass("TRN2", target_bir_lowering=False, debug=False)
    inp = lambda n, s, d=F32: nc.dram_tensor(n, s, d, kind="ExternalInput").ap()
    DBG = bool(os.environ.get("MK_DEBUG"))
    def dram(n, s, d, sh=False):
        if DBG and not sh and not n.startswith("cc_"):
            return nc.dram_tensor(n, s, d, kind="ExternalOutput").ap()
        return nc.dram_tensor(n, s, d, kind="Internal",
                              addr_space=("Shared" if sh else "Local")).ap()

    d_xtp = inp("d_xtp", [HID, TOK])
    d_xtok = inp("d_xtok", [1024, HID])
    d_xres = inp("d_xres", [HID, 1024])
    d_wmt = inp("d_wmt", [HID, FCORE])
    d_convw = inp("d_convw", [544, 4])
    d_convb = inp("d_convb", [544, 1])
    d_dtbias = inp("d_dtbias", [8, 1])
    d_alog = inp("d_alog", [8, 1])
    d_drow = inp("d_drow", [1, 8])
    d_nw4 = inp("d_nw4", [128, 4])
    d_wot = inp("d_wot", [INTER, HID])
    d_gpk = inp("d_gpk", [1024, 2048], I8)
    d_gsc = inp("d_gsc", [64, 2048])
    d_upk = inp("d_upk", [1024, 2048], I8)
    d_usc = inp("d_usc", [64, 2048])
    d_dpk = inp("d_dpk", [1024, 2048], I8)
    d_dsc = inp("d_dsc", [64, 2048])
    d_idf = inp("d_idf", [128, 128])
    d_ones1 = inp("d_ones1", [1, 128])
    d_ut = inp("d_ut", [128, 128])
    d_slbig = inp("d_slbig", [128, 512])
    d_selm = inp("d_selm", [8, 1024])
    d_out = nc.dram_tensor("d_out", [HID, 1024], F32, kind="ExternalOutput").ap()

    d_hsin = dram("d_hsin", [512, TOK], BF16)
    d_hsconv = dram("d_hsconv", [512, TOK], BF16)
    d_gate = dram("d_gate", [512, TOK], BF16)
    d_xn = dram("d_xn", [HID, TOK], BF16)
    d_y = dram("d_y", [512, TOK], BF16)
    d_sown = dram("d_sown", [HID, 1024], F32)
    d_g = dram("d_g", [HID, 1024], BF16)
    d_dtr = dram("d_dtr", [8, TOK], F32)
    cc_st_i = dram("cc_st_i", [16, 128], F32)
    cc_st_o = dram("cc_st_o", [128, 128], F32, sh=True)
    cc_bc_i = dram("cc_bc_i", [32, TOK], BF16)
    cc_bc_o = dram("cc_bc_o", [256, TOK], BF16, sh=True)
    cc_wd_i = dram("cc_wd_i", [16, 2048, 128], BF16)
    cc_wd_o = dram("cc_wd_o", [128, 2048, 128], BF16, sh=True)
    cc_yn_i = dram("cc_yn_i", [INTER, 1024], BF16)
    cc_yn_o = dram("cc_yn_o", [INTER, 1024], BF16)
    cc_ss_i = dram("cc_ss_i", [8, 1024], F32)
    cc_ss_o = dram("cc_ss_o", [8, 1024], F32)
    cc_in_i = dram("cc_in_i", [16384, 1024], BF16)
    cc_in_o = dram("cc_in_o", [16384, 1024], BF16)

    es = contextlib.ExitStack()
    with es:
        sb = lambda n, s, d: es.enter_context(nc.sbuf_tensor(n, s, d))
        idf = sb("idf", [128, 128], F32)
        idbf = sb("idbf", [128, 128], BF16)
        ones1f = sb("ones1f", [1, 128], F32)
        ones8f = sb("ones8f", [8, 128], F32)
        onebf = sb("onebf", [128, 1], BF16)
        utf = sb("utf", [128, 128], F32)
        slbig = sb("slbig", [128, 512], BF16)
        cw5 = sb("cw5", [128, 20], F32)
        cb5 = sb("cb5", [128, 5], F32)
        nw4 = sb("nw4", [128, 4], F32)
        alog = sb("alog", [8, 1], F32)
        bias8h = sb("bias8h", [8, 1], F32)
        drow = sb("drow", [1, 8], F32)
        mrT = sb("mrT", [16, 128], F32)
        cfl = sb("cfl", [128, 8], F32)
        db = sb("db", [128, 8], F32)
        mr8 = sb("mr8", [128, 16], F32)
        pb = [es.enter_context(nc.psum_tensor(f"pb{i}", [128, 512], F32)) for i in range(8)]
        semT = es.enter_context(nc.semaphore("semT"))
        semV = es.enter_context(nc.semaphore("semV"))
        semS = es.enter_context(nc.semaphore("semS"))
        semG = es.enter_context(nc.semaphore("semG"))
        semY = es.enter_context(nc.semaphore("semY"))
        semC = es.enter_context(nc.semaphore("semC"))
        block = es.enter_context(nc.Block())

        cx = Ctx(nc, block)
        cx.ct, cx.cv, cx.cs = Counter(semT), Counter(semV), Counter(semS)
        cx.cg, cx.cy, cx.cc = Counter(semG, 16), Counter(semY, 16), Counter(semC, 1)
        cx.pb = pb
        ct, cv, cs, cg, cy, ccc = cx.ct, cx.cv, cx.cs, cx.cg, cx.cy, cx.cc

        def barrier():
            vals = [(semT, ct.val), (semV, cv.val), (semS, cs.val),
                    (semG, cg.val), (semY, cy.val), (semC, ccc.val)]
            for reg in (cx.T, cx.V, cx.S, cx.G, cx.Y):
                def f(e, vals=tuple(vals)):
                    for sem, v in vals:
                        e.wait_ge(sem, v)
                reg(f)

        def g_c(g):
            cx.dg(g, idbf[:], d_idf)
            cx.dg(g, slbig[:], d_slbig)
            cx.dg(g, ones8f[:], d_ones1.broadcast_to([8, 128]))
            cx.dg(g, nw4[:], d_nw4)
        cx.G(g_c)

        def y_c(sy):
            cx.dy(sy, idf[:], d_idf)
            cx.dy(sy, ones1f[:], d_ones1)
            cx.dy(sy, utf[:], d_ut)
            for r in range(4):
                cx.dy(sy, cw5[:, 4 * r:4 * r + 4], d_convw[128 * r:128 * r + 128, :])
                cx.dy(sy, cb5[:, r:r + 1], d_convb[128 * r:128 * r + 128, :])
            cx.dy(sy, cw5[0:32, 16:20], d_convw[512:544, :])
            cx.dy(sy, cb5[0:32, 4:5], d_convb[512:544, :])
            cx.dy(sy, alog[:], d_alog)
            cx.dy(sy, bias8h[:], d_dtbias)
            cx.dy(sy, drow[:], d_drow)
        cx.Y(y_c)
        consts_g, consts_y = cg.val, cy.val

        def v_one(v):
            v.memset(onebf[:], 1.0)
            v.memset(cfl[:, 0:1], 1e-5)
            v.memset(cfl[:, 1:2], float(INTER) * 1e-5)
            v.memset(cfl[:, 2:3], float(np.log(np.sqrt(INTER))))
            v.memset(cfl[:, 3:4], -0.5)
            cv.inc(v.memset(cfl[:, 4:5], -1.0))
        cx.V(v_one)

        if True:
            pscope = es.enter_context(contextlib.ExitStack())
            xstat = pscope.enter_context(nc.sbuf_tensor("xstat", [128, HID], F32))
            xsq = pscope.enter_context(nc.sbuf_tensor("xsq", [128, HID], F32))
            raws = pscope.enter_context(nc.sbuf_tensor("raws", [128, 16], F32))
            mT = pscope.enter_context(nc.sbuf_tensor("mT", [8, 128], F32))
            qT = pscope.enter_context(nc.sbuf_tensor("qT", [8, 128], F32))
            msq = pscope.enter_context(nc.sbuf_tensor("msq", [8, 128], F32))
            for j in range(8):
                def y_ld(sy, j=j, wv=cv.val, ws=cs.val):
                    if j > 0:
                        sy.wait_ge(semV, wv)
                        sy.wait_ge(semS, ws)
                    cx.dy(sy, xstat[:], d_xtok[128 * j:128 * j + 128, :])
                cx.Y(y_ld)
                ldv = cy.val

                def s_sq(s, ldv=ldv):
                    s.wait_ge(semY, ldv)
                    cs.inc(s.activation(xsq[:], xstat[:], AF.Square))
                cx.S(s_sq)

                def v_red(v, j=j, ldv=ldv, ws=cs.val):
                    v.wait_ge(semY, ldv)
                    v.wait_ge(semS, ws)
                    v.reduce_sum(raws[:, j:j + 1], xstat[:], mybir.AxisListType.X)
                    cv.inc(v.reduce_sum(raws[:, 8 + j:9 + j], xsq[:], mybir.AxisListType.X))
                cx.V(v_red)

            def t_mr(tt, ws=cs.val, wv=cv.val):
                tt.wait_ge(semS, ws)
                tt.wait_ge(semV, wv)
                tt.wait_ge(semY, consts_y)
                tt.matmul(pb[0][:, 0:128][0:8, :], raws[:, 0:8], idf[:], is_transpose=True,
                          start=True, stop=False)
                ct.inc(tt.matmul(pb[0][:, 128:256][0:8, :], raws[:, 8:16], idf[:], is_transpose=True,
                                 start=False, stop=False))
            cx.T(t_mr)

            def v_mr(v, wt=ct.val):
                v.wait_ge(semT, wt)
                v.tensor_scalar(mT[:], pb[0][:, 0:128][0:8, :], 1.0 / HID, None, ALU.mult)
                cv.inc(v.tensor_scalar(qT[:], pb[0][:, 128:256][0:8, :], 1.0 / HID, None, ALU.mult))
            cx.V(v_mr)

            def s_msq(s, wv=cv.val):
                s.wait_ge(semV, wv)
                cs.inc(s.activation(msq[:], mT[:], AF.Square))
            cx.S(s_msq)

            def v_var(v, ws=cs.val):
                v.wait_ge(semS, ws)
                cv.inc(v.tensor_sub(qT[:], qT[:], msq[:]))
            cx.V(v_var)

            def s_r(s, wv=cv.val):
                s.wait_ge(semV, wv)
                s.activation(qT[:], qT[:], AF.Ln, bias=cfl[0:8, 0:1])
                cs.inc(s.activation(qT[:], qT[:], AF.Exp, scale=cfl[0:8, 3:4]))
            cx.S(s_r)

            def y_mr(sy, ws=cs.val):
                sy.wait_ge(semS, ws)
                cx.dy(sy, cc_st_i[0:8, :], mT[:])
                cx.dy(sy, cc_st_i[8:16, :], qT[:])
            cx.Y(y_mr)

            def g_ag1(g, wy=cy.val):
                g.wait_ge(semY, wy)
                ccc.inc(g.collective_compute("AllGather", ALU.bypass,
                                             replica_groups=[list(range(8))],
                                             ins=[cc_st_i.opt()], outs=[cc_st_o.opt()]))
            cx.G(g_ag1)
        pscope.close()
        ag_stats_c = ccc.val
        if DBG:
            d_dbgst = nc.dram_tensor("d_dbgst", [128, 128], F32, kind="ExternalOutput").ap()
            d_dbgmr = nc.dram_tensor("d_dbgmr", [128, 16], F32, kind="ExternalOutput").ap()
            d_dbgmt = nc.dram_tensor("d_dbgmt", [16, 128], F32, kind="ExternalOutput").ap()

            def y_dbg(sy, wc=ag_stats_c):
                sy.wait_ge(semC, wc)
                cx.dy(sy, d_dbgst, cc_st_o)
                cx.dy(sy, d_dbgmr, mr8[:])
                cx.dy(sy, d_dbgmt, mrT[:])
            cx.Y(y_dbg)
        if PH <= 1:
            return nc

        emit_dequant_mat(cx, d_dpk, d_dsc, cc_wd_i, consts_y)

        def g_ag2(g, wy=cy.val, wv=cv.val):
            g.wait_ge(semY, wy)
            g.wait_ge(semV, wv)
            ccc.inc(g.collective_compute("AllGather", ALU.bypass,
                                         replica_groups=[list(range(8))],
                                         ins=[cc_wd_i.opt()], outs=[cc_wd_o.opt()]))
        cx.G(g_ag2)
        ag_wd_c = ccc.val
        barrier()
        if PH <= 2:
            return nc

        if True:
            pscope = es.enter_context(contextlib.ExitStack())
            wmt = pscope.enter_context(nc.sbuf_tensor("wmt", [128, 16 * FCORE], BF16))
            xb0 = pscope.enter_context(nc.sbuf_tensor("xb0", [128, 16 * 512], BF16))
            xb1 = pscope.enter_context(nc.sbuf_tensor("xb1", [128, 16 * 512], BF16))
            xnt = pscope.enter_context(nc.sbuf_tensor("xnt", [128, 16 * 512], BF16))
            tmpf = pscope.enter_context(nc.sbuf_tensor("tmpf", [128, 512], F32))
            Rb = pscope.enter_context(nc.sbuf_tensor("Rb", [128, 512], F32))
            Cb = pscope.enter_context(nc.sbuf_tensor("Cb", [128, 512], F32))
            mrow = pscope.enter_context(nc.sbuf_tensor("mrow", [1, 512], F32))
            rrow = pscope.enter_context(nc.sbuf_tensor("rrow", [1, 512], F32))
            crow = pscope.enter_context(nc.sbuf_tensor("crow", [1, 512], F32))
            stg = pscope.enter_context(nc.sbuf_tensor("stg", [128, 5 * 512], BF16))
            stgdt = pscope.enter_context(nc.sbuf_tensor("stgdt", [8, 512], F32))
            def g_wmt(g):
                cx.dg(g, wmt[:].rearrange("p (c f) -> p c f", c=16), d_wmt.rearrange("(c p) f -> p c f", p=128))
            cx.G(g_wmt)
            wmt_g = cg.val

            for tt in range(16):
                xb = (xb0, xb1)[tt % 2]
                tok0 = 512 * tt
                prev_mm = ct.val
                def g_x(g, tt=tt, xb=xb, tok0=tok0, wt=ct.val):
                    if tt >= 2:
                        g.wait_ge(semT, wt)
                    cx.dg(g, xb[:].rearrange("p (c t) -> p c t", c=16), d_xtp[:, tok0:tok0 + 512].rearrange("(c p) t -> p c t", p=128))
                cx.G(g_x)
                x_g = cg.val

                def y_mr2(sy, tt=tt, wc=ag_stats_c):
                    sy.wait_ge(semC, wc)
                    base = 16 * (tt // 2) + 4 * (tt % 2)
                    cx.dy(sy, mrow[:].rearrange("o (a b) -> o a b", a=4), cc_st_o[base:base + 4, :].unsqueeze(0))
                    cx.dy(sy, rrow[:].rearrange("o (a b) -> o a b", a=4), cc_st_o[base + 8:base + 12, :].unsqueeze(0))
                cx.Y(y_mr2)
                mr_y = cy.val

                def v_cr(v, mr_y=mr_y):
                    v.wait_ge(semY, mr_y)
                    cv.inc(v.scalar_tensor_tensor(crow[:], mrow[:], -1.0, rrow[:],
                                                  ALU.mult, ALU.mult))
                cx.V(v_cr)
                cr_v = cv.val

                def t_bc(tt_, cr_v=cr_v, wv2=cv.val):
                    tt_.wait_ge(semV, cr_v)
                    tt_.matmul(pb[6][:], ones1f[:], rrow[:], start=True, stop=False)
                    ct.inc(tt_.matmul(pb[7][:], ones1f[:], crow[:], start=True, stop=False))
                cx.T(t_bc)
                bc_t = ct.val

                def v_bc(v, bc_t=bc_t):
                    v.wait_ge(semT, bc_t)
                    v.tensor_copy(Rb[:], pb[6][:])
                    cv.inc(v.tensor_copy(Cb[:], pb[7][:]))
                cx.V(v_bc)
                bc_v = cv.val

                def v_xn(v, xb=xb, bc_v=bc_v, x_g=x_g, prev_mm=prev_mm):
                    v.wait_ge(semV, bc_v)
                    v.wait_ge(semG, x_g)
                    v.wait_ge(semT, prev_mm)
                    for c2 in range(16):
                        v.tensor_tensor(tmpf[:], xb[:, 512 * c2:512 * c2 + 512], Rb[:], ALU.mult)
                        ins = v.tensor_add(xnt[:, 512 * c2:512 * c2 + 512], tmpf[:], Cb[:])
                    cv.inc(ins)
                cx.V(v_xn)
                xn_v = cv.val

                def y_xn(sy, tok0=tok0, xn_v=xn_v):
                    sy.wait_ge(semV, xn_v)
                    cx.dy(sy, d_xn[:, tok0:tok0 + 512].rearrange("(c p) t -> p c t", p=128), xnt[:].rearrange("p (c t) -> p c t", c=16))
                cx.Y(y_xn)

                for rnd, fs in ((0, range(0, 5)), (1, range(5, 9))):
                    def t_mm(tt_, fs=fs, rnd=rnd, xn_v=xn_v, wg=wmt_g, wv=cv.val, ws=cs.val):
                        tt_.wait_ge(semV, xn_v)
                        tt_.wait_ge(semG, wg)
                        tt_.wait_ge(semS, ws)
                        for fi, f in enumerate(fs):
                            for c2 in range(16):
                                ins = tt_.matmul(pb[fi][:], wmt[:, FCORE * c2 + 128 * f: FCORE * c2 + 128 * f + 128],
                                                 xnt[:, 512 * c2:512 * c2 + 512],
                                                 start=(c2 == 0), stop=(c2 == 15))
                        ct.inc(ins)
                    cx.T(t_mm)
                    mm_t = ct.val

                    def s_ev(s, fs=fs, rnd=rnd, tt=tt, tok0=tok0, mm_t=mm_t, wy=cy.val):
                        s.wait_ge(semT, mm_t)
                        s.wait_ge(semY, wy)
                        for fi, f in enumerate(fs):
                            if f < 4:
                                ins = s.activation(stg[:, 512 * fi:512 * fi + 512], pb[fi][:], AF.Copy)
                            elif f == 4:
                                ins = s.activation(stg[:, 512 * fi:512 * fi + 512][0:32, :], pb[fi][0:32, :], AF.Copy)
                            else:
                                ins = s.activation(stg[:, 512 * fi:512 * fi + 512], pb[fi][:], AF.Copy)
                        cs.inc(ins)
                    cx.S(s_ev)
                    ev_s = cs.val

                    if rnd == 0:
                        def v_dt(v, tok0=tok0, mm_t=mm_t, wy=cy.val):
                            v.wait_ge(semT, mm_t)
                            v.wait_ge(semY, wy)
                            cv.inc(v.tensor_copy(stgdt[:], pb[4][32:40, :]))
                        cx.V(v_dt)
                        dt_v = cv.val

                        def y_dt(sy, tok0=tok0, dt_v=dt_v):
                            sy.wait_ge(semV, dt_v)
                            cx.dy(sy, d_dtr[:, tok0:tok0 + 512], stgdt[:])
                        cx.Y(y_dt)

                    def y_sp(sy, fs=fs, rnd=rnd, tok0=tok0, ev_s=ev_s):
                        sy.wait_ge(semS, ev_s)
                        for fi, f in enumerate(fs):
                            if f < 4:
                                ins = sy.dma_start(out=d_hsin[128 * f:128 * f + 128, tok0:tok0 + 512],
                                                   in_=stg[:, 512 * fi:512 * fi + 512])
                            elif f == 4:
                                ins = sy.dma_start(out=cc_bc_i[:, tok0:tok0 + 512],
                                                   in_=stg[:, 512 * fi:512 * fi + 512][0:32, :])
                            else:
                                ins = sy.dma_start(out=d_gate[128 * (f - 5):128 * (f - 5) + 128, tok0:tok0 + 512],
                                                   in_=stg[:, 512 * fi:512 * fi + 512])
                            ins.then_inc(semY, 16)
                        cy.val += 16 * len(list(fs))
                    cx.Y(y_sp)
        pscope.close()
        barrier()
        if PH <= 3:
            return nc

        if True:
            pscope = es.enter_context(contextlib.ExitStack())
            cvin = pscope.enter_context(nc.sbuf_tensor("cvin", [128, TOK + 6], BF16))
            cvo = pscope.enter_context(nc.sbuf_tensor("cvo", [128, SEQ], BF16))
            cvstg = pscope.enter_context(nc.sbuf_tensor("cvstg", [128, SEQ], BF16))
            for r in range(5):
                nr = 128 if r < 4 else 32
                src_d = d_hsin if r < 4 else cc_bc_i
                r0 = 128 * r
                def v_z(v, nr=nr):
                    v.wait_ge(semY, cy.val)
                    cv.inc(v.memset(cvin[0:nr, 0:3], 0.0))
                    cv.inc(v.memset(cvin[0:nr, SEQ + 3:SEQ + 6], 0.0))
                cx.V(v_z)
                def y_li(sy, r=r, nr=nr, src_d=src_d, wv=cv.val):
                    sy.wait_ge(semV, wv)
                    if r < 4:
                        cx.dy(sy, cvin[0:nr, 3:SEQ + 3], src_d[128 * r:128 * r + nr, 0:SEQ])
                        cx.dy(sy, cvin[0:nr, SEQ + 6:], src_d[128 * r:128 * r + nr, SEQ:])
                    else:
                        cx.dy(sy, cvin[0:nr, 3:SEQ + 3], src_d[0:nr, 0:SEQ])
                        cx.dy(sy, cvin[0:nr, SEQ + 6:], src_d[0:nr, SEQ:])
                cx.Y(y_li)
                li_y = cy.val
                for b in range(NB):
                    off = b * (SEQ + 3) if b == 0 else SEQ + 6 - 3
                    base = 3 + b * (SEQ + 3)
                    def v_cv(v, r=r, nr=nr, base=base, li_y=li_y, ws=cs.val):
                        v.wait_ge(semY, li_y)
                        v.wait_ge(semS, ws)
                        v.tensor_scalar(cvo[0:nr, :], cvin[0:nr, base - 3:base - 3 + SEQ], cw5[0:nr, 4 * r:4 * r + 1], None, ALU.mult)
                        for k in range(1, 4):
                            ins = v.scalar_tensor_tensor(cvo[0:nr, :], cvin[0:nr, base - 3 + k:base - 3 + k + SEQ],
                                                         cw5[0:nr, 4 * r + k:4 * r + k + 1], cvo[0:nr, :],
                                                         ALU.mult, ALU.add)
                        cv.inc(ins)
                    cx.V(v_cv)
                    def s_si(s, r=r, nr=nr, wv=cv.val, wy2=cy.val):
                        s.wait_ge(semV, wv)
                        s.wait_ge(semY, wy2)
                        cs.inc(s.activation(cvstg[0:nr, :], cvo[0:nr, :], AF.Silu, bias=cb5[0:nr, r:r + 1]))
                    cx.S(s_si)
                    def y_so(sy, r=r, nr=nr, b=b, ws=cs.val):
                        sy.wait_ge(semS, ws)
                        if r < 4:
                            cx.dy(sy, d_hsconv[128 * r:128 * r + nr, b * SEQ:(b + 1) * SEQ], cvstg[0:nr, :])
                        else:
                            cx.dy(sy, cc_bc_i[0:nr, b * SEQ:(b + 1) * SEQ], cvstg[0:nr, :])
                    cx.Y(y_so)

        pscope.close()

        def g_ag3(g, wy=cy.val):
            g.wait_ge(semY, wy)
            ccc.inc(g.collective_compute("AllGather", ALU.bypass,
                                         replica_groups=[list(range(8))],
                                         ins=[cc_bc_i.opt()], outs=[cc_bc_o.opt()]))
        cx.G(g_ag3)
        ag_bc_c = ccc.val
        barrier()
        if PH <= 4:
            return nc

        if True:
            pscope = es.enter_context(contextlib.ExitStack())
            Bres = pscope.enter_context(nc.sbuf_tensor("Bres", [128, TOK], BF16))
            Cres = pscope.enter_context(nc.sbuf_tensor("Cres", [128, TOK], BF16))
            hs4a = pscope.enter_context(nc.sbuf_tensor("hs4a", [128, 4 * LCH], BF16))
            hs4b = pscope.enter_context(nc.sbuf_tensor("hs4b", [128, 4 * LCH], BF16))
            ysta = pscope.enter_context(nc.sbuf_tensor("ysta", [128, 512], BF16))
            ystb = pscope.enter_context(nc.sbuf_tensor("ystb", [128, 512], BF16))
            T16 = pscope.enter_context(nc.sbuf_tensor("T16", [128, 16], F32))
            ca = pscope.enter_context(nc.sbuf_tensor("ca", [128, 8], F32))
            caTT = pscope.enter_context(nc.sbuf_tensor("caTT", [8, 128], F32))
            bias8 = pscope.enter_context(nc.sbuf_tensor("bias8", [128, 8], F32))
            F2e = pscope.enter_context(nc.sbuf_tensor("F2e", [128, 8], F32))
            F2 = pscope.enter_context(nc.sbuf_tensor("F2", [128, 8], F32))
            epos = pscope.enter_context(nc.sbuf_tensor("epos", [128, 8], F32))
            ecaL = pscope.enter_context(nc.sbuf_tensor("ecaL", [128, 8], F32))
            CBTT = pscope.enter_context(nc.sbuf_tensor("CBTT", [128, 128], BF16))
            BT = pscope.enter_context(nc.sbuf_tensor("BT", [128, 128], BF16))
            hsT = pscope.enter_context(nc.sbuf_tensor("hsT", [128, 512], BF16))
            dtxw = pscope.enter_context(nc.sbuf_tensor("dtxw", [128, 512], BF16))
            M4 = pscope.enter_context(nc.sbuf_tensor("M4", [128, 1024], BF16))
            G4 = pscope.enter_context(nc.sbuf_tensor("G4", [128, 1024], BF16))
            t1s = pscope.enter_context(nc.sbuf_tensor("t1s", [128, 512], F32))
            Ysb = pscope.enter_context(nc.sbuf_tensor("Ysb", [128, 512], BF16))
            Ssc = pscope.enter_context(nc.sbuf_tensor("Ssc", [128, 512], F32))
            Sbf = pscope.enter_context(nc.sbuf_tensor("Sbf", [128, 512], BF16))
            rhsD = pscope.enter_context(nc.sbuf_tensor("rhsD", [8, 1024], F32))
            id8 = pscope.enter_context(nc.sbuf_tensor("id8", [128, 8 * 128], BF16))
            def y_bc(sy, wc=ag_bc_c):
                sy.wait_ge(semC, wc)
                for a in range(8):
                    cx.dy(sy, Bres[16 * a:16 * a + 16, :], cc_bc_o[32 * a:32 * a + 16, :])
                    cx.dy(sy, Cres[16 * a:16 * a + 16, :], cc_bc_o[32 * a + 16:32 * a + 32, :])
            cx.Y(y_bc)

            t = dict(TOKL=TOK, SEQL=SEQ, hs4=None, Bres=Bres, Cres=Cres, y4=None,
                     alog=alog, bias8h=bias8h, drow=drow, db=db,
                     id8=id8, idf=idf, idbf=idbf, ones1f=ones1f, ones8f=ones8f,
                     utf=utf, slbig=slbig, T16=T16, ca=ca, caTT=caTT,
                     bias8=bias8, F2e=F2e, F2=F2, epos=epos, ecaL=ecaL, CBTT=CBTT,
                     BT=BT, hsT=hsT, dtxw=dtxw, M4=M4, G4=G4, t1=t1s, Ysb=Ysb,
                     S=Ssc, Sbf=Sbf, rhsD=rhsD)
            t["acol"] = acol2 = pscope.enter_context(nc.sbuf_tensor("acol2", [8, 1], F32))
            t["cm1"] = cfl[:, 4:5]
            selm = pscope.enter_context(nc.sbuf_tensor("selm", [8, 1024], F32))
            t["selm"] = selm
            Tdtr = pscope.enter_context(nc.sbuf_tensor("Tdtr", [8, TOK], F32))
            Tlnd = pscope.enter_context(nc.sbuf_tensor("Tlnd", [8, TOK], F32))
            Ta8 = pscope.enter_context(nc.sbuf_tensor("Ta8", [8, TOK], F32))
            t["dtraw"] = Tdtr
            t["dt"] = Tdtr
            t["sp1"] = Tlnd
            t["lndt8"] = Tlnd
            t["a8"] = Ta8

            def y_selm(sy):
                cx.dy(sy, selm[:], d_selm)
                cx.dy(sy, Tdtr[:], d_dtr)
            cx.Y(y_selm)
            t["prep_wait_v"] = cv.val
            t["prep_wait_g"] = cg.val
            t["prep_wait_y"] = cy.val
            cx.serial = True
            emit_scan_prep(cx, t)
            if DBG:
                d_dba = nc.dram_tensor("d_dba", [8, TOK], F32, kind="ExternalOutput").ap()
                d_dbl = nc.dram_tensor("d_dbl", [8, TOK], F32, kind="ExternalOutput").ap()
                d_dbd = nc.dram_tensor("d_dbd", [8, TOK], F32, kind="ExternalOutput").ap()
                d_dbB = nc.dram_tensor("d_dbB", [128, TOK], F32, kind="ExternalOutput").ap()

                def y_dba(sy, ws=cs.val, wv=cv.val, wy=cy.val):
                    sy.wait_ge(semS, ws)
                    sy.wait_ge(semV, wv)
                    sy.wait_ge(semY, wy)
                    cx.dy(sy, d_dba, Ta8[:])
                    cx.dy(sy, d_dbl, Tlnd[:])
                    cx.dy(sy, d_dbd, Tdtr[:])
                cx.Y(y_dba)

                def g_dbB(g, wy=cy.val):
                    g.wait_ge(semY, wy)
                    cx.dg(g, d_dbB, Bres[:])
                cx.G(g_dbB)
            emit_scan_stream(cx, t, d_hsconv, d_y, hs4a, hs4b, ysta, ystb)
            cx.serial = False
        pscope.close()
        barrier()
        if PH <= 5:
            return nc

        if True:
            pscope = es.enter_context(contextlib.ExitStack())
            yrow = pscope.enter_context(nc.sbuf_tensor("yrow", [128, TOK], BF16))
            grow = pscope.enter_context(nc.sbuf_tensor("grow", [128, TOK], BF16))
            sgrow = pscope.enter_context(nc.sbuf_tensor("sgrow", [128, TOK], BF16))
            ynq = pscope.enter_context(nc.sbuf_tensor("ynq", [128, TOK], BF16))
            ssacc = pscope.enter_context(nc.sbuf_tensor("ssacc", [1, TOK], F32))
            def v_ssz(v):
                cv.inc(v.memset(ssacc[:], 0.0))
            cx.V(v_ssz)
            for r in range(4):
                def y_ly(sy, r=r, ws=cs.val, wv=cv.val):
                    sy.wait_ge(semS, ws)
                    sy.wait_ge(semV, wv)
                    cx.dy(sy, yrow[:], d_y[128 * r:128 * r + 128, :])
                    cx.dy(sy, grow[:], d_gate[128 * r:128 * r + 128, :])
                cx.Y(y_ly)
                ly_y = cy.val

                def s_sg(s, ly_y=ly_y):
                    s.wait_ge(semY, ly_y)
                    cs.inc(s.activation(sgrow[:], grow[:], AF.Silu))
                cx.S(s_sg)

                def v_yg(v, ws=cs.val):
                    v.wait_ge(semS, ws)
                    cv.inc(v.tensor_tensor(yrow[:], yrow[:], sgrow[:], ALU.mult))
                cx.V(v_yg)
                yg_v = cv.val

                def s_q(s, yg_v=yg_v, wt=ct.val):
                    s.wait_ge(semV, yg_v)
                    s.wait_ge(semT, wt)
                    cs.inc(s.activation(ynq[:], yrow[:], AF.Square))
                cx.S(s_q)
                q_s = cs.val

                def v_nw(v, r=r, q_s=q_s):
                    v.wait_ge(semS, q_s)
                    cv.inc(v.tensor_scalar(yrow[:], yrow[:], nw4[:, r:r + 1], None, ALU.mult))
                cx.V(v_nw)
                nw_v = cv.val

                def y_wyn(sy, r=r, nw_v=nw_v):
                    sy.wait_ge(semV, nw_v)
                    o = cc_yn_i.rearrange("(j f) n -> f j n", j=8)[128 * r:128 * r + 128]
                    cx.dy(sy, o, yrow[:].rearrange("p (j n) -> p j n", j=8))
                cx.Y(y_wyn)

                for tt2 in range(16):
                    bank = tt2 % 2
                    def t_ss(tt_, r=r, tt2=tt2, bank=bank, q_s=q_s, wv=cv.val):
                        tt_.wait_ge(semS, q_s)
                        if r > 0 or tt2 > 1:
                            tt_.wait_ge(semV, wv)
                        ct.inc(tt_.matmul(pb[bank][0:1, :], onebf[:], ynq[:, 512 * tt2:512 * tt2 + 512],
                                          start=True, stop=False))
                    cx.T(t_ss)
                    ss_t = ct.val

                    def v_ss(v, r=r, tt2=tt2, bank=bank, ss_t=ss_t):
                        v.wait_ge(semT, ss_t)
                        cv.inc(v.tensor_add(ssacc[:, 512 * tt2:512 * tt2 + 512],
                                            ssacc[:, 512 * tt2:512 * tt2 + 512], pb[bank][0:1, :]))
                    cx.V(v_ss)

            def y_wss(sy, wv=cv.val):
                sy.wait_ge(semV, wv)
                cx.dy(sy, cc_ss_i.rearrange("j n -> (j n)").unsqueeze(0), ssacc[:])
            cx.Y(y_wss)

            def g_ag4(g, wy=cy.val):
                g.wait_ge(semY, wy)
                from concourse.aot_env import allow_ici_all_to_all
                with allow_ici_all_to_all():
                    ccc.inc(g.collective_compute("AllToAll", ALU.bypass,
                                                 replica_groups=[list(range(8))],
                                                 ins=[cc_yn_i.opt()], outs=[cc_yn_o.opt()]))
                    ccc.inc(g.collective_compute("AllToAll", ALU.bypass,
                                                 replica_groups=[list(range(8))],
                                                 ins=[cc_ss_i.opt()], outs=[cc_ss_o.opt()]))
            cx.G(g_ag4)
            ag_yn_c = ccc.val
        pscope.close()
        barrier()
        if PH <= 6:
            return nc

        if True:
            pscope = es.enter_context(contextlib.ExitStack())
            wgh = pscope.enter_context(nc.sbuf_tensor("wgh", [128, 16 * 1024], BF16))
            wuh = pscope.enter_context(nc.sbuf_tensor("wuh", [128, 16 * 1024], BF16))
            xs0 = pscope.enter_context(nc.sbuf_tensor("xs0", [128, 16 * 512], BF16))
            xs1 = pscope.enter_context(nc.sbuf_tensor("xs1", [128, 16 * 512], BF16))
            gel = pscope.enter_context(nc.sbuf_tensor("gel", [128, 8 * 512], BF16))
            istg = pscope.enter_context(nc.sbuf_tensor("istg", [128, 8 * 512], BF16))
            des = pscope.enter_context(contextlib.ExitStack())
            dsb = lambda n, s, d: des.enter_context(nc.sbuf_tensor(n, s, d))
            pkt2 = dsb("q2_pk", [128, 2048], I16)
            St2 = dsb("q2_st", [128, 2048], BF16)
            tmp2 = dict(nlo=dsb("q2_nlo", [128, 2048], I16), nhi=dsb("q2_nhi", [128, 2048], I16),
                        u=dsb("q2_u", [128, 2048], I16), sgi=dsb("q2_sgi", [128, 2048], I16),
                        t1=dsb("q2_t1", [128, 2048], BF16), sgf=dsb("q2_sgf", [128, 2048], BF16),
                        ssg=dsb("q2_ssg", [128, 2048], BF16))
            for half in range(2):
                f0 = 1024 * half
                for mat, (dpk2, dsc2, wdst) in enumerate(
                        ((d_gpk, d_gsc, wgh), (d_upk, d_usc, wuh))):
                    for j in range(8):
                        def g_l2(g, j=j, dpk2=dpk2, dsc2=dsc2, f0=f0, wv=cv.val):
                            g.wait_ge(semV, wv)
                            cx.dg(g, pkt2[:, 0:1024], dpk2[128 * j:128 * j + 128, f0:f0 + 1024])
                            for a in range(8):
                                cx.dg(g, St2[16 * a:16 * a + 16, 0:1024], dsc2[8 * j + a:8 * j + a + 1, f0:f0 + 1024].broadcast_to([16, 1024]))
                        cx.G(g_l2)
                        l2_g = cg.val

                        def v_dq2(v, j=j, wdst=wdst, l2_g=l2_g, wt=ct.val):
                            v.wait_ge(semG, l2_g)
                            v.wait_ge(semT, wt)
                            tvs = dict(nlo=tmp2["nlo"][:, 0:1024], nhi=tmp2["nhi"][:, 0:1024],
                                       u=tmp2["u"][:, 0:1024], sgi=tmp2["sgi"][:, 0:1024],
                                       t1=tmp2["t1"][:, 0:1024], sgf=tmp2["sgf"][:, 0:1024],
                                       ssg=tmp2["ssg"][:, 0:1024])
                            emit_dequant_tile2(v, pkt2[:, 0:1024], St2[:, 0:1024],
                                               wdst[:, 1024 * (2 * j):1024 * (2 * j) + 1024],
                                               wdst[:, 1024 * (2 * j + 1):1024 * (2 * j + 1) + 1024], tvs)
                            cv.inc(v.engine_nop())
                        cx.V(v_dq2)
                dq_v2 = cv.val
                if PH <= 6.2:
                    continue

                for tt in range(16):
                    xs = (xs0, xs1)[tt % 2]
                    tok0 = 512 * tt
                    def g_xn2(g, xs=xs, tok0=tok0, wt=ct.val):
                        if tt >= 2:
                            g.wait_ge(semT, wt)
                        cx.dg(g, xs[:].rearrange("p (c n) -> p c n", c=16), d_xn[:, tok0:tok0 + 512].rearrange("(c p) n -> p c n", p=128))
                    cx.G(g_xn2)
                    xn_g = cg.val

                    def t_g(tt_, xs=xs, dq_v2=dq_v2, xn_g=xn_g, ws=cs.val, wv=cv.val):
                        tt_.wait_ge(semV, max(dq_v2, wv))
                        tt_.wait_ge(semG, xn_g)
                        tt_.wait_ge(semS, ws)
                        for f in range(8):
                            for c2 in range(16):
                                ins = tt_.matmul(pb[f][:], wgh[:, 1024 * c2 + 128 * f:1024 * c2 + 128 * f + 128],
                                                 xs[:, 512 * c2:512 * c2 + 512],
                                                 start=(c2 == 0), stop=(c2 == 15))
                        ct.inc(ins)
                    cx.T(t_g)
                    tg_t = ct.val

                    def s_gel(s, tg_t=tg_t, wv=cv.val):
                        s.wait_ge(semT, tg_t)
                        s.wait_ge(semV, wv)
                        for f in range(8):
                            ins = s.activation(gel[:, 512 * f:512 * f + 512], pb[f][:], AF.Gelu_apprx_tanh)
                        cs.inc(ins)
                    cx.S(s_gel)
                    gel_s = cs.val

                    def t_u(tt_, xs=xs, gel_s=gel_s):
                        tt_.wait_ge(semS, gel_s)
                        for f in range(8):
                            for c2 in range(16):
                                ins = tt_.matmul(pb[f][:], wuh[:, 1024 * c2 + 128 * f:1024 * c2 + 128 * f + 128],
                                                 xs[:, 512 * c2:512 * c2 + 512],
                                                 start=(c2 == 0), stop=(c2 == 15))
                        ct.inc(ins)
                    cx.T(t_u)
                    tu_t = ct.val

                    def v_int(v, tu_t=tu_t, wy=cy.val):
                        v.wait_ge(semT, tu_t)
                        v.wait_ge(semY, wy)
                        for f in range(8):
                            ins = v.tensor_tensor(istg[:, 512 * f:512 * f + 512],
                                                  gel[:, 512 * f:512 * f + 512], pb[f][:], ALU.mult)
                        cv.inc(ins)
                    cx.V(v_int)
                    int_v = cv.val

                    def y_int(sy, half=half, tt=tt, int_v=int_v):
                        sy.wait_ge(semV, int_v)
                        o = cc_in_i.rearrange("(j q) n -> q j n", j=8)[1024 * half:1024 * half + 1024,
                                                                      tt // 2, 512 * (tt % 2):512 * (tt % 2) + 512]
                        cx.dy(sy, o.rearrange("(f p) n -> p f n", p=128), istg[:].rearrange("p (f n) -> p f n", f=8))
                    cx.Y(y_int)

            if PH <= 6.5:
                return nc

            def g_ag5(g, wy=cy.val):
                g.wait_ge(semY, wy)
                from concourse.aot_env import allow_ici_all_to_all
                with allow_ici_all_to_all():
                    ccc.inc(g.collective_compute("AllToAll", ALU.bypass,
                                                 replica_groups=[list(range(8))],
                                                 ins=[cc_in_i.opt()], outs=[cc_in_o.opt()]))
            cx.G(g_ag5)
            ag_in_c = ccc.val
        pscope.close()
        barrier()
        if PH <= 7:
            return nc

        if True:
            pscope = es.enter_context(contextlib.ExitStack())
            ynown = pscope.enter_context(nc.sbuf_tensor("ynown", [128, 32 * 1024], BF16))
            wo0 = pscope.enter_context(nc.sbuf_tensor("wo0", [128, 32 * 128], BF16))
            wo1 = pscope.enter_context(nc.sbuf_tensor("wo1", [128, 32 * 128], BF16))
            Rsb = pscope.enter_context(nc.sbuf_tensor("Rsb", [128, 1024], F32))
            rsr = pscope.enter_context(nc.sbuf_tensor("rsr", [1, 1024], F32))
            ssg8 = pscope.enter_context(nc.sbuf_tensor("ssg8", [8, 1024], F32))
            cstg = pscope.enter_context(nc.sbuf_tensor("cstg", [128, 512], F32))
            def y_lc(sy, wc=ag_yn_c):
                sy.wait_ge(semC, wc)
                cx.dy(sy, ynown[:].rearrange("p (k n) -> p k n", k=32), cc_yn_o.rearrange("(k p) n -> p k n", p=128))
                cx.dy(sy, ssg8[:], cc_ss_o)
            cx.Y(y_lc)
            lc_y = cy.val

            def t_v(tt_, lc_y=lc_y):
                tt_.wait_ge(semY, lc_y)
                tt_.matmul(pb[4][0:1, 0:512], ones8f[:, 0:1], ssg8[:, 0:512], start=True, stop=False)
                ct.inc(tt_.matmul(pb[5][0:1, 0:512], ones8f[:, 0:1], ssg8[:, 512:1024], start=True, stop=False))
            cx.T(t_v)
            tv_t = ct.val

            def s_rs(s, tv_t=tv_t):
                s.wait_ge(semT, tv_t)
                s.activation(rsr[:, 0:512], pb[4][0:1, 0:512], AF.Ln, bias=cfl[0:1, 1:2])
                s.activation(rsr[:, 512:1024], pb[5][0:1, 0:512], AF.Ln, bias=cfl[0:1, 1:2])
                s.activation(rsr[:, 0:512], rsr[:, 0:512], AF.Exp, scale=cfl[0:1, 3:4], bias=cfl[0:1, 2:3])
                cs.inc(s.activation(rsr[:, 512:1024], rsr[:, 512:1024], AF.Exp, scale=cfl[0:1, 3:4], bias=cfl[0:1, 2:3]))
            cx.S(s_rs)
            rs_s = cs.val

            def t_rb(tt_, rs_s=rs_s):
                tt_.wait_ge(semS, rs_s)
                tt_.matmul(pb[2][:], ones1f[:], rsr[:, 0:512], start=True, stop=False)
                ct.inc(tt_.matmul(pb[3][:], ones1f[:], rsr[:, 512:1024], start=True, stop=False))
            cx.T(t_rb)
            rb_t = ct.val

            def v_rb(v, rb_t=rb_t):
                v.wait_ge(semT, rb_t)
                v.tensor_copy(Rsb[:, 0:512], pb[2][:])
                cv.inc(v.tensor_copy(Rsb[:, 512:1024], pb[3][:]))
            cx.V(v_rb)
            rb_v = cv.val

            for h in range(16):
                wob = (wo0, wo1)[h % 2]
                def g_wo(g, h=h, wob=wob, wt=ct.val):
                    if h >= 2:
                        g.wait_ge(semT, wt)
                    cx.dg(g, wob[:].rearrange("p (k n) -> p k n", k=32), d_wot[:, 128 * h:128 * h + 128].rearrange("(k p) n -> p k n", p=128))
                cx.G(g_wo)
                wo_g = cg.val

                def t_op(tt_, h=h, wob=wob, wo_g=wo_g, lc_y=lc_y, wv=cv.val, ws=cs.val):
                    tt_.wait_ge(semG, wo_g)
                    tt_.wait_ge(semY, lc_y)
                    tt_.wait_ge(semV, wv)
                    for q in range(2):
                        for k in range(32):
                            ins = tt_.matmul(pb[q][:], wob[:, 128 * k:128 * k + 128],
                                             ynown[:, 1024 * k + 512 * q:1024 * k + 512 * q + 512],
                                             start=(k == 0), stop=(k == 31))
                    ct.inc(ins)
                cx.T(t_op)
                op_t = ct.val

                for q in range(2):
                    def v_ev(v, h=h, q=q, op_t=op_t, wy=cy.val):
                        v.wait_ge(semT, op_t)
                        v.wait_ge(semY, wy)
                        cv.inc(v.tensor_tensor(cstg[:], pb[q][:], Rsb[:, 512 * q:512 * q + 512], ALU.mult))
                    cx.V(v_ev)
                    ev_v = cv.val

                    def y_so2(sy, h=h, q=q, ev_v=ev_v):
                        sy.wait_ge(semV, ev_v)
                        cx.dy(sy, d_sown[128 * h:128 * h + 128, 512 * q:512 * q + 512], cstg[:])
                    cx.Y(y_so2)
        pscope.close()
        barrier()
        if PH <= 8:
            return nc

        if True:
            pscope = es.enter_context(contextlib.ExitStack())
            interq = pscope.enter_context(nc.sbuf_tensor("interq", [128, 128 * 256], BF16))
            wd0 = pscope.enter_context(nc.sbuf_tensor("wd0", [128, 128 * 128], BF16))
            wd1 = pscope.enter_context(nc.sbuf_tensor("wd1", [128, 128 * 128], BF16))
            gstg = pscope.enter_context(nc.sbuf_tensor("gstg", [128, 256], BF16))
            sfin = pscope.enter_context(nc.sbuf_tensor("sfin", [128, 16 * 128], F32))
            xfin = pscope.enter_context(nc.sbuf_tensor("xfin", [128, 16 * 128], F32))
            gfin = pscope.enter_context(nc.sbuf_tensor("gfin", [128, 16 * 128], BF16))
            semO = es.enter_context(nc.semaphore("semO"))
            co = Counter(semO, 16)
            for q in range(4):
                def y_iq(sy, q=q, wc=ag_in_c, wt=ct.val):
                    sy.wait_ge(semC, wc)
                    if q > 0:
                        sy.wait_ge(semT, wt)
                    cx.dy(sy, interq[:].rearrange("p (k n) -> p k n", k=128), cc_in_o[:, 256 * q:256 * q + 256].rearrange("(k p) n -> p k n", p=128))
                cx.Y(y_iq)
                iq_y = cy.val

                for h in range(16):
                    wdb = (wd0, wd1)[h % 2]
                    def g_wd(g, q=q, h=h, wdb=wdb, wc=ag_wd_c, wt=ct.val):
                        g.wait_ge(semC, wc)
                        if q + h >= 2:
                            g.wait_ge(semT, wt)
                        for rk in range(8):
                            cx.dg(g, wdb[:, 2048 * rk:2048 * rk + 2048].rearrange("p (j n) -> p j n", j=16), cc_wd_o[16 * rk + h].rearrange("(j p) h2 -> p j h2", p=128))
                    cx.G(g_wd)
                    wd_g = cg.val

                    def t_dn(tt_, q=q, h=h, wdb=wdb, wd_g=wd_g, iq_y=iq_y, ws=cs.val):
                        tt_.wait_ge(semG, wd_g)
                        tt_.wait_ge(semY, iq_y)
                        tt_.wait_ge(semS, ws)
                        bank = h % 4
                        for k in range(128):
                            ins = tt_.matmul(pb[bank][:, 0:256], wdb[:, 128 * k:128 * k + 128],
                                             interq[:, 256 * k:256 * k + 256],
                                             start=(k == 0), stop=(k == 127))
                        ct.inc(ins)
                    cx.T(t_dn)
                    dn_t = ct.val

                    def s_g(s, q=q, h=h, dn_t=dn_t, wy=cy.val):
                        s.wait_ge(semT, dn_t)
                        s.wait_ge(semY, wy)
                        cs.inc(s.activation(gstg[:], pb[h % 4][:, 0:256], AF.Silu))
                    cx.S(s_g)
                    gs_s = cs.val

                    def y_g(sy, q=q, h=h, gs_s=gs_s):
                        sy.wait_ge(semS, gs_s)
                        cx.dy(sy, d_g[128 * h:128 * h + 128, 256 * q:256 * q + 256], gstg[:])
                    cx.Y(y_g)

            for u in range(8):
                u0 = 128 * u
                def y_fl(sy, u0=u0, wv=cv.val, wy=cy.val):
                    sy.wait_ge(semV, wv)
                    sy.wait_ge(semY, wy)
                    cx.dy(sy, sfin[:].rearrange("p (k n) -> p k n", k=16), d_sown[:, u0:u0 + 128].rearrange("(k p) n -> p k n", p=128))
                    cx.dy(sy, xfin[:].rearrange("p (k n) -> p k n", k=16), d_xres[:, u0:u0 + 128].rearrange("(k p) n -> p k n", p=128))
                    cx.dy(sy, gfin[:].rearrange("p (k n) -> p k n", k=16), d_g[:, u0:u0 + 128].rearrange("(k p) n -> p k n", p=128))
                cx.Y(y_fl)
                fl_y = cy.val

                def v_fin(v, fl_y=fl_y, wo=co.val):
                    v.wait_ge(semY, fl_y)
                    v.wait_ge(semO, wo)
                    v.tensor_tensor(sfin[:], sfin[:], gfin[:], ALU.mult)
                    cv.inc(v.tensor_add(sfin[:], sfin[:], xfin[:]))
                cx.V(v_fin)
                fin_v = cv.val

                def y_out2(sy, u0=u0, fin_v=fin_v):
                    sy.wait_ge(semV, fin_v)
                    sy.dma_start(out=d_out[:, u0:u0 + 128].rearrange("(k p) n -> p k n", p=128),
                                 in_=sfin[:].rearrange("p (k n) -> p k n", k=16)).then_inc(semO, 16)
                cx.Y(y_out2)
                co.val += 16

            def y_fin(sy, wo=co.val):
                sy.wait_ge(semO, wo)
            cx.Y(y_fin)
            pscope.close()
    return nc


def kernel(**inputs):
    in_maps = host_prep(inputs)
    nc = build_kernel()
    res = run_bass_kernel_spmd(nc, in_maps, core_ids=list(range(NCORE)))
    return host_finish(res.results)
